# version 09
# baseline (speedup 1.0000x reference)
import dataclasses
from contextlib import ExitStack

import numpy as np
import ml_dtypes
import concourse.bass as bass
import concourse.mybir as mybir
from concourse.bass_utils import run_bass_kernel_spmd

F32 = mybir.dt.float32
BF16 = mybir.dt.bfloat16
U32 = mybir.dt.uint32
AF = mybir.ActivationFunctionType
ALU = mybir.AluOpType
AX = mybir.AxisListType.X
ds = bass.ds

P = 128
B = 2
T = 8192
C = 2048
NCH = C // P
QW = 512
NQB = T // QW
NKT = T // P
SCALE = 1.0 / float(np.sqrt(128.0))
HALF = 1024
NTT = 8
KEXP = 8
RK = 64
EPS = 1e-5


class Sem:
    __slots__ = ("h", "n")

    def __init__(self, nc, name):
        self.h = nc.alloc_semaphore(name)
        self.n = 0

    def inc(self, ins, k=1):
        ins.then_inc(self.h, k)
        self.n += k
        return self.n


STOP = "full"
SUB2B = 99


def build_program(stop=None, sub2b=None, sub2c=99):
    stop = stop or STOP
    sub = SUB2B if sub2b is None else sub2b
    subc = sub2c
    order = ["qkv0", "c0", "att00", "ph1", "coll", "2a", "2b", "2c", "full"]
    lvl = order.index(stop)
    nc = bass.Bass("TRN2", target_bir_lowering=False)

    xT = nc.dram_tensor("xT", [B, C, T], F32, kind="ExternalInput").ap()
    wq = nc.dram_tensor("wq", [C, 256], F32, kind="ExternalInput").ap()
    wk = nc.dram_tensor("wk", [C, 256], F32, kind="ExternalInput").ap()
    wv = nc.dram_tensor("wv", [C, 256], F32, kind="ExternalInput").ap()
    wo = nc.dram_tensor("wo", [C, C], F32, kind="ExternalInput").ap()
    w1 = nc.dram_tensor("w1", [C, C], F32, kind="ExternalInput").ap()
    b1m = nc.dram_tensor("b1m", [P, NCH], F32, kind="ExternalInput").ap()
    w2 = nc.dram_tensor("w2", [C, KEXP], F32, kind="ExternalInput").ap()
    b2 = nc.dram_tensor("b2", [1, KEXP], F32, kind="ExternalInput").ap()
    wv2 = nc.dram_tensor("wv2", [C, 512], F32, kind="ExternalInput").ap()
    wu = nc.dram_tensor("wu", [512, C], F32, kind="ExternalInput").ap()
    xres = nc.dram_tensor("xres", [2048, C], F32, kind="ExternalInput").ap()
    info = nc.dram_tensor("info", [1, 4], U32, kind="ExternalInput").ap()
    out = nc.dram_tensor("out", [2048, C], F32, kind="ExternalOutput").ap()
    dbg = nc.dram_tensor("dbg", [P, 3 * 2 * T], F32, kind="ExternalOutput").ap()
    dbg2 = nc.dram_tensor("dbg2", [512, T], mybir.dt.bfloat16,
                          kind="ExternalOutput").ap()

    y_in = nc.dram_tensor("y_in", [512, T], BF16, kind="Internal").ap()
    y_all = nc.dram_tensor("y_all", [8 * 512, T], BF16, kind="Internal",
                           addr_space="Shared").ap()
    at_scr = nc.dram_tensor("at_scr", [KEXP, QW], BF16, kind="Internal").ap()

    ident_f = nc.inline_tensor(np.eye(P, dtype=np.float32), "ident_f").ap()
    onescol_d = nc.inline_tensor(np.ones((P, 1), dtype=ml_dtypes.bfloat16),
                                 "onescol").ap()
    onesrow_d = nc.inline_tensor(np.ones((1, P), dtype=np.float32), "onesrow").ap()
    e0_np = np.zeros((1, P), dtype=ml_dtypes.bfloat16); e0_np[0, :RK] = 1.0
    e1_np = np.zeros((1, P), dtype=ml_dtypes.bfloat16); e1_np[0, RK:] = 1.0
    e0_d = nc.inline_tensor(e0_np, "e0c").ap()
    e1_d = nc.inline_tensor(e1_np, "e1c").ap()
    eps_d = nc.inline_tensor(np.full((P, 1), EPS, dtype=np.float32), "epsc").ap()

    ident_s = nc.alloc_sbuf_tensor("idents", [P, P], F32)
    onescol = nc.alloc_sbuf_tensor("onescol_s", [P, 1], BF16)
    onesrow = nc.alloc_sbuf_tensor("onesrow_s", [1, P], F32)
    e0_s = nc.alloc_sbuf_tensor("e0_s", [1, P], BF16)
    e1_s = nc.alloc_sbuf_tensor("e1_s", [1, P], BF16)
    w2_sb = nc.alloc_sbuf_tensor("w2_sb", [P, NCH * KEXP], BF16)
    b1_sb = nc.alloc_sbuf_tensor("b1_sb", [P, NCH], F32)
    b2_sb = nc.alloc_sbuf_tensor("b2_sb", [1, KEXP], F32)
    info_s = nc.alloc_sbuf_tensor("info_s", [1, 4], U32)
    eps_sb = nc.alloc_sbuf_tensor("eps_sb", [P, 1], F32)
    wo_sb = [nc.alloc_sbuf_tensor(f"wo_sb{i}", [P, NCH * 512], BF16)
             for i in range(2)]

    pb = [nc.alloc_psum_tensor(f"pb{i}", [P, 512], F32) for i in range(8)]

    S = lambda nm: Sem(nc, nm)
    d_const = S("d_const"); c_msk = S("c_msk"); d_xt = S("d_xt")
    c_qkv = S("c_qkv"); c_evac = S("c_evac")
    c_sq = S("c_sq"); c_nm = S("c_nm"); c_rmax = S("c_rmax")
    c_mx = S("c_mx"); c_cs = S("c_cs"); c_cneg = S("c_cneg")
    c_sT = S("c_sT"); c_E = S("c_E"); c_Ea = S("c_Ea"); c_AV = S("c_AV")
    c_rl = S("c_rl"); c_rlb = S("c_rlb"); c_cp = S("c_cp"); c_yev = S("c_yev")
    d_yin = S("d_yin"); c_coll = S("c_coll"); bar = S("bar")
    d_w = S("d_w"); d_y = S("d_y"); d_x = S("d_x")
    c_pe2 = S("c_pe2"); c_dv2 = S("c_dv2")
    c_s2 = S("c_s2"); c_s1 = S("c_s1"); c_lnA = S("c_lnA"); c_lnB = S("c_lnB")
    c_ln = S("c_ln")
    c_tr = S("c_tr"); c_trE = S("c_trE")
    c_w1 = S("c_w1"); c_hrel = S("c_hrel")
    c_vxc = S("c_vxc"); c_vx = S("c_vx")
    c_al = S("c_al"); c_sm = S("c_sm"); c_alpha = S("c_alpha")
    c_at = S("c_at"); c_atE = S("c_atE"); d_at = S("d_at")
    c_ab = S("c_ab"); c_wvx = S("c_wvx")
    c_ffn = S("c_ffn"); c_r2 = S("c_r2"); d_out = S("d_out")

    NBAR = [0]
    LNK = [0]

    with nc.Block() as blk:

        def barrier():
            tgt = 5 * (NBAR[0] + 1)

            def mk():
                def f(eng):
                    eng.nop().then_inc(bar.h, 1)
                    eng.wait_ge(bar.h, tgt)
                return f
            blk.gpsimd(mk()); blk.sync(mk()); blk.tensor(mk())
            blk.scalar(mk()); blk.vector(mk())
            bar.n = tgt
            NBAR[0] += 1

        stack = ExitStack()
        ph1 = ExitStack()

        def pt(name, shape, dt):
            return ph1.enter_context(nc.sbuf_tensor(name, shape, dt))

        maskb = pt("maskb", [P, 1024], BF16)
        cneg128 = [pt(f"cneg128_{b}", [P, 1], F32) for b in range(B)]
        wq_sb = pt("wq_sb", [P, NCH * 256], BF16)
        wk_sb = pt("wk_sb", [P, NCH * 256], BF16)
        wv_sb = pt("wv_sb", [P, NCH * 256], BF16)
        xt_sb = [pt(f"xt_sb{i}", [P, NCH * 512], BF16) for i in range(2)]
        q_sb = pt("q_sb", [P, 2 * T], BF16)
        k_sb = pt("k_sb", [P, 2 * T], BF16)
        v_sb = pt("v_sb", [P, NKT * 256], BF16)
        e_sb = [pt(f"e_sb{i}", [P, QW], BF16) for i in range(2)]
        ystage = [pt(f"ystage{i}", [P, QW], BF16) for i in range(2)]
        rl_sb = pt("rl_sb", [1, QW], F32)
        rlb_sb = pt("rlb_sb", [P, QW], F32)
        nsq = [pt(f"nsq{i}", [P, 512], BF16) for i in range(2)]
        maxrow = pt("maxrow", [1, 64], F32)
        mq2 = pt("mq2", [1, 1], F32)
        mk2 = pt("mk2", [1, 1], F32)
        m2 = pt("m2", [1, 1], F32)
        t1s = pt("t1s", [1, 1], F32)
        cval = pt("cval", [1, 1], F32)
        cneg1 = pt("cneg1", [1, 1], F32)

        def g_start(g):
            for (dst, src) in ((ident_s, ident_f), (onescol, onescol_d),
                               (onesrow, onesrow_d), (e0_s, e0_d), (e1_s, e1_d),
                               (info_s, info), (b2_sb, b2), (b1_sb, b1m),
                               (eps_sb, eps_d)):
                g.dma_start(out=dst[:], in_=src[:]).then_inc(d_const.h, 16)
                d_const.n += 16
            g.dma_start(out=w2_sb[:].rearrange("p (n k) -> p n k", k=KEXP),
                        in_=w2.rearrange("(n p) k -> p n k", p=P)).then_inc(
                d_const.h, 16)
            d_const.n += 16
            c_msk.inc(g.memset(maskb[:], 1.0))
            g.wait_ge(c_msk.h, c_msk.n)
            c_msk.inc(g.affine_select(maskb[:], maskb[:], [[1, 1024]], ALU.is_ge,
                                      0.0, base=-512, channel_multiplier=-1))
            for (dst, src) in ((wq_sb, wq), (wk_sb, wk), (wv_sb, wv)):
                g.dma_start(out=dst[:].rearrange("p (n h) -> p n h", h=256),
                            in_=src.rearrange("(n p) h -> p n h", p=P)).then_inc(
                    d_const.h, 16)
                d_const.n += 16
        blk.gpsimd(g_start)

        regs = {}

        def sy_start(sy, tgt=None):
            sy.wait_ge(d_const.h, 10 * 16)
            r1 = sy.alloc_register("r_yb0")
            r2 = sy.alloc_register("r_yb1")
            sy.reg_load(r1, info_s[0:1, 0:1])
            sy.reg_load(r2, info_s[0:1, 1:2])
            ymax = 256 * T + (T - HALF)
            regs["yb0"] = sy.snap(r1, min_val=0, max_val=ymax)
            regs["yb1"] = sy.snap(r2, min_val=0, max_val=ymax)
        blk.sync(sy_start)

        NBATCH = 1 if lvl < 3 else B
        for b in range(NBATCH):
            base_xt = d_xt.n
            base_ev = c_evac.n

            def g_qkv(g, b=b):
                src = xT[b].rearrange("(n p) t -> p n t", p=P)
                for tb in range(NQB):
                    gtb = b * NQB + tb
                    if gtb >= 2:
                        g.wait_ge(c_qkv.h, 8 * (gtb - 1))
                    dst = xt_sb[tb % 2][:].rearrange("p (n t) -> p n t", t=512)
                    for hlf in range(2):
                        g.dma_start(out=dst[:, hlf * 8:(hlf + 1) * 8, :],
                                    in_=src[:, hlf * 8:(hlf + 1) * 8,
                                            tb * 512:(tb + 1) * 512]).then_inc(
                            d_xt.h, 16)
                        d_xt.n += 16
            blk.gpsimd(g_qkv)

            def t_qkv(t, b=b, base_xt=base_xt, base_ev=base_ev):
                if b == 0:
                    t.wait_ge(d_const.h, 13 * 16)
                for tb in range(NQB):
                    gtb = b * NQB + tb
                    t.wait_ge(d_xt.h, base_xt + 32 * (tb + 1))
                    xt = xt_sb[tb % 2]
                    ch = 0
                    for (wsb, bank0) in ((wq_sb, 0), (wk_sb, 2)):
                        for hh in range(2):
                            bank = bank0 + hh
                            if gtb >= 1:
                                t.wait_ge(c_evac.h, 8 * (gtb - 1) + ch + 1)
                            for ci in range(NCH):
                                ins = t.matmul(pb[bank][:, :],
                                               wsb[:, ci * 256 + hh * 128:
                                                   ci * 256 + hh * 128 + 128],
                                               xt[:, ci * 512:(ci + 1) * 512],
                                               start=(ci == 0),
                                               stop=(ci == NCH - 1))
                            c_qkv.inc(ins)
                            ch += 1
                    for tk in range(4):
                        bank = 4 + tk
                        if gtb >= 1:
                            t.wait_ge(c_evac.h, 8 * (gtb - 1) + ch + 1)
                        for ci in range(NCH):
                            ins = t.matmul(pb[bank][:, 0:256],
                                           xt[:, ci * 512 + tk * 128:
                                              ci * 512 + tk * 128 + 128],
                                           wv_sb[:, ci * 256:(ci + 1) * 256],
                                           start=(ci == 0),
                                           stop=(ci == NCH - 1))
                        c_qkv.inc(ins)
                        ch += 1
            blk.tensor(t_qkv)

            def a_qkv(a, b=b):
                for tb in range(NQB):
                    gtb = b * NQB + tb
                    ch = 0
                    for (dst_sb, bank0) in ((q_sb, 0), (k_sb, 2)):
                        for hh in range(2):
                            a.wait_ge(c_qkv.h, 8 * gtb + ch + 1)
                            c_evac.inc(a.copy(
                                dst_sb[:, hh * T + tb * 512:
                                       hh * T + (tb + 1) * 512],
                                pb[bank0 + hh][:, :]))
                            ch += 1
                    for tk in range(4):
                        a.wait_ge(c_qkv.h, 8 * gtb + ch + 1)
                        kt = tb * 4 + tk
                        c_evac.inc(a.copy(v_sb[:, kt * 256:(kt + 1) * 256],
                                          pb[4 + tk][:, 0:256]))
                        ch += 1
            blk.scalar(a_qkv)

            barrier()

            if lvl < 1:
                break
            base_sq = c_sq.n
            base_nm = c_nm.n
            base_rm = c_rmax.n
            srcs = [(q_sb, 0), (q_sb, 1), (k_sb, 0), (k_sb, 1)]

            def a_cb(a, base_nm=base_nm):
                idx = 0
                for (sb_, hh) in srcs:
                    for chk in range(NCH):
                        if idx >= 2:
                            a.wait_ge(c_nm.h, base_nm + idx - 1)
                        c_sq.inc(a.activation(
                            nsq[idx % 2][:, :],
                            sb_[:, hh * T + chk * 512: hh * T + (chk + 1) * 512],
                            AF.Square))
                        idx += 1
            blk.scalar(a_cb)

            def t_cb(t, base_sq=base_sq, base_rm=base_rm):
                idx = 0
                for _ in range(64):
                    t.wait_ge(c_sq.h, base_sq + idx + 1)
                    if idx >= 2:
                        t.wait_ge(c_rmax.h, base_rm + idx - 1)
                    c_nm.inc(t.matmul(pb[idx % 2][0:1, :], onescol[:],
                                      nsq[idx % 2][:, :], start=True, stop=True))
                    idx += 1
            blk.tensor(t_cb)

            def v_cb(v, base_nm=base_nm):
                idx = 0
                for si in range(4):
                    for chk in range(NCH):
                        v.wait_ge(c_nm.h, base_nm + idx + 1)
                        col = si * 16 + chk
                        c_rmax.inc(v.reduce_max(maxrow[0:1, col:col + 1],
                                                pb[idx % 2][0:1, :], axis=AX))
                        idx += 1
                v.wait_ge(c_rmax.h, c_rmax.n)
                c_mx.inc(v.reduce_max(mq2[:], maxrow[0:1, 0:32], axis=AX))
                v.wait_ge(c_mx.h, c_mx.n)
                c_mx.inc(v.reduce_max(mk2[:], maxrow[0:1, 32:64], axis=AX))
                v.wait_ge(c_mx.h, c_mx.n)
                c_mx.inc(v.tensor_mul(m2[:], mq2[:], mk2[:]))
            blk.vector(v_cb)

            def a_cb2(a):
                a.wait_ge(c_mx.h, c_mx.n)
                c_cs.inc(a.activation(t1s[:], m2[:], AF.Ln))
                a.wait_ge(c_cs.h, c_cs.n)
                c_cs.inc(a.activation(cval[:], t1s[:], AF.Exp, scale=0.5))
            blk.scalar(a_cb2)

            def v_cb2(v):
                v.wait_ge(c_cs.h, c_cs.n)
                c_mx.inc(v.tensor_scalar_mul(cneg1[:], cval[:], -float(SCALE)))
            blk.vector(v_cb2)

            def t_cb2(t):
                t.wait_ge(c_mx.h, c_mx.n)
                c_nm.inc(t.matmul(pb[2][:, 0:1], onesrow[:], cneg1[:],
                                  start=True, stop=True))
            blk.tensor(t_cb2)

            def v_cb3(v, b=b):
                v.wait_ge(c_nm.h, c_nm.n)
                c_cneg.inc(v.tensor_copy(cneg128[b][:], pb[2][:, 0:1]))
            blk.vector(v_cb3)

            barrier()

            if lvl < 2:
                break
            for hh in range(1 if lvl == 2 else 2):
                iters = [(qb, kt) for qb in range(NQB) for kt in range(4 * qb + 4)]
                base_sT = c_sT.n
                base_E = c_E.n
                base_Ea = c_Ea.n
                base_AV = c_AV.n
                base_rl = c_rl.n
                base_rlb = c_rlb.n
                base_cp = c_cp.n
                base_yev = c_yev.n
                base_yin = d_yin.n

                def t_att(t, b=b, hh=hh, iters=iters, base_E=base_E,
                          base_rl=base_rl, base_yev=base_yev, base_cp=base_cp):
                    qoff = hh * T
                    n = len(iters)

                    def issue_sT(j):
                        qbj, ktj = iters[j]
                        c_sT.inc(t.matmul(
                            pb[j % 2][:, :],
                            k_sb[:, qoff + ktj * 128: qoff + (ktj + 1) * 128],
                            q_sb[:, qoff + qbj * 512: qoff + (qbj + 1) * 512],
                            start=True, stop=True))

                    if base_E > 0:
                        t.wait_ge(c_E.h, base_E)
                    issue_sT(0)
                    for i, (qb, kt) in enumerate(iters):
                        par = i % 2
                        if i + 1 < n:
                            if i >= 1:
                                t.wait_ge(c_E.h, base_E + i)
                            issue_sT(i + 1)
                        t.wait_ge(c_E.h, base_E + i + 1)
                        first = (kt == 0)
                        last = (kt == 4 * qb + 3)
                        if first and qb >= 2:
                            t.wait_ge(c_yev.h, base_yev + qb - 1)
                        elif first and base_yev > 0:
                            t.wait_ge(c_yev.h, base_yev)
                        t.matmul(pb[4 + qb % 2][0:1, :], onescol[:],
                                 e_sb[par][:, :], start=first, stop=last)
                        c_AV.inc(t.matmul(pb[2 + qb % 2][:, :],
                                          v_sb[:, kt * 256 + hh * 128:
                                               kt * 256 + hh * 128 + 128],
                                          e_sb[par][:, :], start=first,
                                          stop=last))
                        if last:
                            t.wait_ge(c_rl.h, base_rl + qb + 1)
                            if qb >= 1:
                                t.wait_ge(c_cp.h, base_cp + qb)
                            elif base_cp > 0:
                                t.wait_ge(c_cp.h, base_cp)
                            c_rlb.inc(t.matmul(pb[6][:, :], onesrow[:],
                                               rl_sb[:], start=True, stop=True))
                blk.tensor(t_att)

                def a_att(a, b=b, iters=iters, base_sT=base_sT, base_AV=base_AV):
                    a.wait_ge(c_cneg.h, b + 1)
                    for i, (qb, kt) in enumerate(iters):
                        par = i % 2
                        a.wait_ge(c_sT.h, base_sT + i + 1)
                        if i >= 2:
                            a.wait_ge(c_AV.h, base_AV + i - 1)
                        elif base_AV > 0:
                            a.wait_ge(c_AV.h, base_AV)
                        ins = a.activation(e_sb[par][:, :], pb[par][:, :], AF.Exp,
                                           bias=cneg128[b][:], scale=float(SCALE))
                        if kt >= 4 * qb:
                            c_Ea.inc(ins)
                        else:
                            c_E.inc(ins)
                blk.scalar(a_att)

                def v_att(v, iters=iters, base_Ea=base_Ea, base_AV=base_AV,
                          base_rlb=base_rlb, base_yin=base_yin):
                    nd = 0
                    for i, (qb, kt) in enumerate(iters):
                        par = i % 2
                        if kt >= 4 * qb:
                            o = (kt - 4 * qb) * 128
                            v.wait_ge(c_Ea.h, base_Ea + nd + 1)
                            nd += 1
                            c_E.inc(v.tensor_mul(e_sb[par][:, :], e_sb[par][:, :],
                                                 maskb[:, 512 - o: 1024 - o]))
                        if kt == 4 * qb + 3:
                            v.wait_ge(c_AV.h, base_AV + i + 1)
                            c_rl.inc(v.reciprocal(rl_sb[:],
                                                  pb[4 + qb % 2][0:1, :]))
                            v.wait_ge(c_rlb.h, base_rlb + qb + 1)
                            c_cp.inc(v.tensor_copy(rlb_sb[:], pb[6][:, :]))
                            v.wait_ge(c_cp.h, c_cp.n)
                            if qb >= 2:
                                v.wait_ge(d_yin.h, base_yin + 16 * (qb - 1))
                            elif base_yin > 0:
                                v.wait_ge(d_yin.h, base_yin)
                            c_yev.inc(v.tensor_mul(ystage[qb % 2][:, :],
                                                   pb[2 + qb % 2][:, :],
                                                   rlb_sb[:]))
                blk.vector(v_att)

                def sy_att(sy, b=b, hh=hh, base_yev=base_yev):
                    for qb in range(NQB):
                        sy.wait_ge(c_yev.h, base_yev + qb + 1)
                        sy.dma_start(
                            out=y_in[b * 256 + hh * 128: b * 256 + (hh + 1) * 128,
                                     qb * 512:(qb + 1) * 512],
                            in_=ystage[qb % 2][:, :]).then_inc(d_yin.h, 16)
                        d_yin.n += 16
                blk.sync(sy_att)

            barrier()

        if lvl < 4:
            def g_dump(g):
                if lvl == 0:
                    g.dma_start(out=dbg[:, 0:2 * T], in_=q_sb[:]).then_inc(d_out.h, 16)
                    g.dma_start(out=dbg[:, 2 * T:4 * T], in_=k_sb[:]).then_inc(d_out.h, 16)
                    g.dma_start(out=dbg[:, 4 * T:6 * T], in_=v_sb[:]).then_inc(d_out.h, 16)
                    d_out.n += 48
                elif lvl == 1:
                    g.dma_start(out=dbg[:, 0:1], in_=cneg128[0][:], allow_slow_non_contiguous=True).then_inc(d_out.h, 16)
                    g.dma_start(out=dbg[0:1, 100:164], in_=maxrow[:]).then_inc(d_out.h, 16)
                    d_out.n += 32
                else:
                    g.dma_start(out=dbg2[:], in_=y_in[:]).then_inc(d_out.h, 16)
                    d_out.n += 16
                g.wait_ge(d_out.h, d_out.n)
            blk.gpsimd(g_dump)
            return nc

        def g_wo_pre(g):
            src = wo.rearrange("(n p) c -> p n c", p=P)
            for par in range(2):
                dst = wo_sb[par][:].rearrange("p (n c) -> p n c", c=512)
                for hlf in range(2):
                    g.dma_start(out=dst[:, hlf * 8:(hlf + 1) * 8, :],
                                in_=src[:, hlf * 8:(hlf + 1) * 8,
                                        par * 512:(par + 1) * 512]).then_inc(
                        d_w.h, 16)
                    d_w.n += 16
        blk.gpsimd(g_wo_pre)

        def g_coll(g):
            g.wait_ge(d_yin.h, d_yin.n)
            g.collective_compute(
                "AllGather", ALU.bypass, replica_groups=[list(range(8))],
                ins=[y_in[:]], outs=[y_all[:]],
            ).then_inc(c_coll.h, 1)
            c_coll.n += 1
        blk.gpsimd(g_coll)

        barrier()
        if lvl == 4:
            def g_dump(g):
                g.wait_ge(c_coll.h, 1)
                g.dma_start(out=dbg2[:], in_=y_all[1024:1536, :]).then_inc(d_out.h, 16)
                d_out.n += 16
                g.wait_ge(d_out.h, d_out.n)
            blk.gpsimd(g_dump)
            return nc
        ph1.close()

        h1 = stack.enter_context(nc.sbuf_tensor("h1", [P, NTT * C], F32))
        wvx = stack.enter_context(nc.sbuf_tensor("wvx", [P, 4 * HALF], BF16))
        alpha = stack.enter_context(nc.sbuf_tensor("alpha", [P, NTT * KEXP], F32))
        s1c = stack.enter_context(nc.sbuf_tensor("s1c", [P, 1], F32))
        s2c = stack.enter_context(nc.sbuf_tensor("s2c", [P, 1], F32))
        meanc = stack.enter_context(nc.sbuf_tensor("meanc", [P, 1], F32))
        msqc = stack.enter_context(nc.sbuf_tensor("msqc", [P, 1], F32))
        varc = stack.enter_context(nc.sbuf_tensor("varc", [P, 1], F32))
        lnc = stack.enter_context(nc.sbuf_tensor("lnc", [P, 1], F32))
        rstdc = stack.enter_context(nc.sbuf_tensor("rstdc", [P, 1], F32))
        negm = stack.enter_context(nc.sbuf_tensor("negm", [P, 1], F32))
        expal = stack.enter_context(nc.sbuf_tensor("expal", [P, KEXP], F32))
        sume = stack.enter_context(nc.sbuf_tensor("sume", [P, 1], F32))
        rse = stack.enter_context(nc.sbuf_tensor("rse", [P, 1], F32))

        def ln_tile(tile_ap, ready_sem, ready_val, sqs_ap):
            k = LNK[0]
            LNK[0] += 1

            def a_ln(a):
                a.wait_ge(ready_sem.h, ready_val)
                if k >= 1:
                    a.wait_ge(c_lnA.h, 3 * k)
                c_s2.inc(a.activation(sqs_ap, tile_ap, AF.Square,
                                      accum_out=s2c[:]))
            blk.scalar(a_ln)

            def v_ln1(v):
                v.wait_ge(ready_sem.h, ready_val)
                if k >= 1:
                    v.wait_ge(c_lnA.h, 3 * k)
                c_s1.inc(v.reduce_sum(s1c[:], tile_ap, axis=AX))
                v.wait_ge(c_s1.h, k + 1)
                v.wait_ge(c_s2.h, k + 1)
                c_lnA.inc(v.tensor_scalar_mul(meanc[:], s1c[:], 1.0 / C))
                v.wait_ge(c_lnA.h, 3 * k + 1)
                c_lnA.inc(v.tensor_mul(msqc[:], meanc[:], meanc[:]))
                v.wait_ge(c_lnA.h, 3 * k + 2)
                c_lnA.inc(v.scalar_tensor_tensor(varc[:], s2c[:], 1.0 / C,
                                                 msqc[:], ALU.mult, ALU.subtract))
            blk.vector(v_ln1)

            def a_ln2(a):
                a.wait_ge(c_lnA.h, 3 * k + 3)
                if k >= 1:
                    a.wait_ge(c_ln.h, k)
                c_lnB.inc(a.activation(lnc[:], varc[:], AF.Ln, bias=eps_sb[:]))
                a.wait_ge(c_lnB.h, 2 * k + 1)
                c_lnB.inc(a.activation(rstdc[:], lnc[:], AF.Exp, scale=-0.5))
            blk.scalar(a_ln2)

            def v_ln2(v):
                v.wait_ge(c_lnB.h, 2 * k + 2)
                c_ln.inc(v.tensor_scalar(tile_ap, tile_ap, meanc[:], rstdc[:],
                                         ALU.subtract, ALU.mult))
            blk.vector(v_ln2)

        for half in range(1 if lvl < 8 else 2):
            colreg_name = "yb0" if half == 0 else "yb1"

            sc2a = ExitStack()
            yh = sc2a.enter_context(nc.sbuf_tensor(f"yh{half}", [P, NCH * HALF],
                                                   BF16))
            xcb = [sc2a.enter_context(nc.sbuf_tensor(f"xcb{half}_{i}",
                                                     [P, NTT * 512], F32))
                   for i in range(2)]
            sqs_a = sc2a.enter_context(nc.sbuf_tensor(f"sqs_a{half}", [P, C],
                                                      BF16))

            base_y = d_y.n
            base_wo = d_w.n if half == 1 else 0
            base_x = d_x.n
            base_pe = c_pe2.n
            base_dv = c_dv2.n
            base_ln1 = c_ln.n

            def sy_2a(sy, half=half, colreg_name=colreg_name, base_dv=base_dv):
                for cblk in range(2):
                    src = xres[half * HALF:(half + 1) * HALF,
                               cblk * 512:(cblk + 1) * 512].rearrange(
                        "(tt p) c -> p tt c", p=P)
                    dst = xcb[cblk % 2][:].rearrange("p (tt c) -> p tt c", c=512)
                    sy.dma_start(out=dst, in_=src).then_inc(d_x.h, 16)
                    d_x.n += 16
                sy.wait_ge(c_coll.h, 1)
                rr = sy.alloc_register(f"r_yb_{half}")
                sy.reg_load(rr, info_s[0:1, half:half + 1])
                ybreg = sy.snap(rr, min_val=0, max_val=256 * T + T - HALF)
                for hl in range(2):
                    src = bass.AP(y_all.tensor, ybreg + hl * 128 * T,
                                  [[T, P], [512 * T, 8], [1, HALF]])
                    dst = yh[:].rearrange("p (r hl t) -> p hl r t", hl=2,
                                          t=HALF)[:, hl]
                    sy.dma_start(out=dst, in_=src).then_inc(d_y.h, 16)
                    d_y.n += 16
                for cblk in range(2, 4):
                    src = xres[half * HALF:(half + 1) * HALF,
                               cblk * 512:(cblk + 1) * 512].rearrange(
                        "(tt p) c -> p tt c", p=P)
                    dst = xcb[cblk % 2][:].rearrange("p (tt c) -> p tt c", c=512)
                    sy.wait_ge(c_dv2.h, base_dv + 8 * (cblk - 1))
                    sy.dma_start(out=dst, in_=src).then_inc(d_x.h, 16)
                    d_x.n += 16
            blk.sync(sy_2a)

            if half == 0:
                def g_2a(g, base_pe=base_pe):
                    src = wo.rearrange("(n p) c -> p n c", p=P)
                    for cblk in (2, 3):
                        par = cblk % 2
                        g.wait_ge(c_pe2.h, base_pe + 8 * (cblk - 1))
                        dst = wo_sb[par][:].rearrange("p (n c) -> p n c", c=512)
                        for hlf in range(2):
                            g.dma_start(out=dst[:, hlf * 8:(hlf + 1) * 8, :],
                                        in_=src[:, hlf * 8:(hlf + 1) * 8,
                                                cblk * 512:(cblk + 1) * 512]
                                        ).then_inc(d_w.h, 16)
                            d_w.n += 16
                blk.gpsimd(g_2a)
                wo_ready = [64, 64, 96, 128]
            else:
                def g_2a(g, base_pe=base_pe, base_wo=base_wo):
                    src = wo.rearrange("(n p) c -> p n c", p=P)
                    for cblk in range(4):
                        par = cblk % 2
                        if cblk >= 2:
                            g.wait_ge(c_pe2.h, base_pe + 8 * (cblk - 1))
                        dst = wo_sb[par][:].rearrange("p (n c) -> p n c", c=512)
                        for hlf in range(2):
                            g.dma_start(out=dst[:, hlf * 8:(hlf + 1) * 8, :],
                                        in_=src[:, hlf * 8:(hlf + 1) * 8,
                                                cblk * 512:(cblk + 1) * 512]
                                        ).then_inc(d_w.h, 16)
                            d_w.n += 16
                blk.gpsimd(g_2a)
                wo_ready = [base_wo + 32, base_wo + 64, base_wo + 96,
                            base_wo + 128]

            def t_2a(t, base_y=base_y, base_dv=base_dv, wo_ready=wo_ready):
                t.wait_ge(d_y.h, base_y + 32)
                for cblk in range(4):
                    par = cblk % 2
                    t.wait_ge(d_w.h, wo_ready[cblk])
                    for tt in range(NTT):
                        n = cblk * NTT + tt
                        if n >= 2:
                            t.wait_ge(c_dv2.h, base_dv + n - 1)
                        for ci in range(NCH):
                            ins = t.matmul(pb[n % 2][:, :],
                                           yh[:, ci * HALF + tt * 128:
                                              ci * HALF + tt * 128 + 128],
                                           wo_sb[par][:, ci * 512:(ci + 1) * 512],
                                           start=(ci == 0),
                                           stop=(ci == NCH - 1))
                        c_pe2.inc(ins)
            blk.tensor(t_2a)

            def v_2a(v, base_pe=base_pe, base_x=base_x):
                for cblk in range(4):
                    for tt in range(NTT):
                        n = cblk * NTT + tt
                        v.wait_ge(c_pe2.h, base_pe + n + 1)
                        v.wait_ge(d_x.h, base_x + 16 * (cblk + 1))
                        c_dv2.inc(v.tensor_add(
                            h1[:, tt * C + cblk * 512: tt * C + (cblk + 1) * 512],
                            pb[n % 2][:, :],
                            xcb[cblk % 2][:, tt * 512:(tt + 1) * 512]))
            blk.vector(v_2a)

            for tt in range(NTT):
                ln_tile(h1[:, tt * C:(tt + 1) * C], c_dv2,
                        base_dv + 3 * NTT + tt + 1, sqs_a[:])

            barrier()
            if lvl == 5:
                def g_dump(g):
                    g.dma_start(out=dbg[:, 0:NTT * C], in_=h1[:]).then_inc(d_out.h, 16)
                    d_out.n += 16
                    g.wait_ge(d_out.h, d_out.n)
                blk.gpsimd(g_dump)
                return nc
            sc2a.close()

            sc2b = ExitStack()
            h1T = sc2b.enter_context(nc.sbuf_tensor(f"h1T{half}",
                                                    [P, NCH * HALF], BF16))
            hm = sc2b.enter_context(nc.sbuf_tensor(f"hm{half}",
                                                   [P, NCH * 512], BF16))
            vx = sc2b.enter_context(nc.sbuf_tensor(f"vx{half}", [P, 4 * 512],
                                                   F32))
            w1_sb = [sc2b.enter_context(nc.sbuf_tensor(f"w1_{half}_{i}",
                                                       [P, NCH * 256], BF16))
                     for i in range(2)]
            wv2_sb = sc2b.enter_context(nc.sbuf_tensor(f"wv2_{half}",
                                                       [P, NCH * 512], BF16))
            at8 = sc2b.enter_context(nc.sbuf_tensor(f"at8_{half}", [KEXP, 512],
                                                    BF16))
            atf = sc2b.enter_context(nc.sbuf_tensor(f"atf_{half}",
                                                    [1, KEXP * 512], BF16))

            base_tr = c_tr.n
            base_trE = c_trE.n
            base_w1 = c_w1.n
            base_hrel = c_hrel.n
            base_vxc = c_vxc.n
            base_vx = c_vx.n
            base_al = c_al.n
            base_sm = c_sm.n
            base_alpha = c_alpha.n
            base_at = c_at.n
            base_atE = c_atE.n
            base_dat = d_at.n
            base_ab = c_ab.n
            base_wvx = c_wvx.n
            base_w2b = d_w.n

            def g_2b(g, base_hrel=base_hrel):
                src = wv2.rearrange("(n p) r -> p n r", p=P)
                dst = wv2_sb[:].rearrange("p (n r) -> p n r", r=512)
                for hlf in range(2):
                    g.dma_start(out=dst[:, hlf * 8:(hlf + 1) * 8, :],
                                in_=src[:, hlf * 8:(hlf + 1) * 8, :]).then_inc(
                        d_w.h, 16)
                    d_w.n += 16
                srcw = w1.rearrange("(n p) e -> p n e", p=P)
                for tqb in range(2 if sub >= 2 else 0):
                    for es in range(8):
                        j = tqb * 8 + es
                        par = j % 2
                        if j >= 2:
                            g.wait_ge(c_hrel.h, base_hrel + 2 * (j - 1))
                        dstw = w1_sb[par][:].rearrange("p (n e) -> p n e", e=256)
                        for hlf in range(2):
                            g.dma_start(out=dstw[:, hlf * 8:(hlf + 1) * 8, :],
                                        in_=srcw[:, hlf * 8:(hlf + 1) * 8,
                                                 es * 256:(es + 1) * 256]
                                        ).then_inc(d_w.h, 16)
                            d_w.n += 16
            blk.gpsimd(g_2b)

            def t_2b(t, half=half, base_ln1=base_ln1, base_trE=base_trE,
                     base_w2b=base_w2b, base_hrel=base_hrel, base_vx=base_vx,
                     base_sm=base_sm, base_alpha=base_alpha, base_atE=base_atE,
                     base_dat=base_dat, base_wvx=base_wvx, base_E=None):
                for tt in range(NTT):
                    t.wait_ge(c_ln.h, base_ln1 + tt + 1)
                    for ci in range(NCH):
                        n = tt * NCH + ci
                        if n >= 2:
                            t.wait_ge(c_trE.h, base_trE + n - 1)
                        c_tr.inc(t.transpose(pb[2 + n % 2][:, 0:128],
                                             h1[:, tt * C + ci * 128:
                                                tt * C + (ci + 1) * 128],
                                             ident_s[:]))
                for tqb in range(2):
                    if sub < 2:
                        break
                    t.wait_ge(c_trE.h, base_trE + 64 * (tqb + 1))
                    for es in range(8):
                        j = tqb * 8 + es
                        t.wait_ge(d_w.h, base_w2b + 32 + 32 * (j + 1))
                        for esub in range(2):
                            ec = es * 2 + esub
                            j2 = tqb * 16 + ec
                            if j2 >= 2:
                                t.wait_ge(c_hrel.h, base_hrel + j2 - 1)
                            if sub >= 6 and tqb == 1 and ec < 2:
                                t.wait_ge(c_wvx.h, base_wvx + 4 * tqb)
                            for ci in range(NCH):
                                ins = t.matmul(
                                    pb[j2 % 2][:, :],
                                    w1_sb[j % 2][:, ci * 256 + esub * 128:
                                                 ci * 256 + esub * 128 + 128],
                                    h1T[:, ci * HALF + tqb * 512:
                                        ci * HALF + tqb * 512 + 512],
                                    start=(ci == 0), stop=(ci == NCH - 1))
                            c_w1.inc(ins)
                    if sub < 3:
                        continue
                    t.wait_ge(d_w.h, base_w2b + 32)
                    for pc in range(4):
                        j3 = tqb * 4 + pc
                        if j3 >= 2:
                            t.wait_ge(c_vx.h, base_vx + j3 - 1)
                        for ci in range(NCH):
                            ins = t.matmul(pb[4 + j3 % 2][:, :],
                                           wv2_sb[:, ci * 512 + pc * 128:
                                                  ci * 512 + pc * 128 + 128],
                                           h1T[:, ci * HALF + tqb * 512:
                                               ci * HALF + tqb * 512 + 512],
                                           start=(ci == 0),
                                           stop=(ci == NCH - 1))
                        c_vxc.inc(ins)
                    if sub < 4:
                        continue
                    t.wait_ge(c_hrel.h, base_hrel + 16 * (tqb + 1))
                    for ttl in range(4):
                        tt = tqb * 4 + ttl
                        if tt >= 2:
                            t.wait_ge(c_sm.h, base_sm + 3 * (tt - 2) + 2)
                        for ci in range(NCH):
                            t.matmul(pb[6 + tt % 2][:, 0:KEXP],
                                     hm[:, ci * 512 + ttl * 128:
                                        ci * 512 + ttl * 128 + 128],
                                     w2_sb[:, ci * KEXP:(ci + 1) * KEXP],
                                     start=(ci == 0), stop=False)
                        c_al.inc(t.matmul(pb[6 + tt % 2][:, 0:KEXP], onesrow[:],
                                          b2_sb[:], start=False, stop=True,
                                          skip_group_check=True))
                    if sub < 5:
                        continue
                    t.wait_ge(c_alpha.h, base_alpha + 4 * (tqb + 1))
                    t.wait_ge(c_trE.h, base_trE + 128)
                    if tqb == 1:
                        t.wait_ge(c_atE.h, base_atE + 1)
                    for ttl in range(4):
                        tt = tqb * 4 + ttl
                        c_at.inc(t.transpose(
                            pb[2][0:KEXP, ttl * 128: ttl * 128 + 128],
                            alpha[:, tt * KEXP:(tt + 1) * KEXP], ident_s[:]))
                    if sub < 6:
                        continue
                    t.wait_ge(d_at.h, base_dat + 32 * (tqb + 1))
                    t.wait_ge(c_hrel.h, base_hrel + 16 * (tqb + 1))
                    for pc in range(4):
                        j4 = tqb * 4 + pc
                        if j4 >= 2:
                            t.wait_ge(c_wvx.h, base_wvx + j4 - 1)
                        t.matmul(pb[pc % 2][:, :], e0_s[:],
                                 atf[0:1, (2 * pc) * 512:(2 * pc) * 512 + 512],
                                 start=True, stop=False, skip_group_check=True)
                        c_ab.inc(t.matmul(pb[pc % 2][:, :], e1_s[:],
                                          atf[0:1, (2 * pc + 1) * 512:
                                              (2 * pc + 1) * 512 + 512],
                                          start=False, stop=True,
                                          skip_group_check=True))
            blk.tensor(t_2b)

            def a_2b(a, base_w1=base_w1, base_al=base_al, base_sm=base_sm):
                for tqb in range(2 if sub >= 2 else 0):
                    for ec in range(16 if sub >= 2 else 0):
                        j2 = tqb * 16 + ec
                        a.wait_ge(c_w1.h, base_w1 + j2 + 1)
                        if sub >= 4 and tqb >= 1 and ec == 0:
                            a.wait_ge(c_al.h, base_al + 4 * tqb)
                        c_hrel.inc(a.activation(
                            hm[:, ec * 512: ec * 512 + 512],
                            pb[j2 % 2][:, :], AF.Relu,
                            bias=b1_sb[:, ec:ec + 1]))
                    for ttl in range(4 if sub >= 4 else 0):
                        tt = tqb * 4 + ttl
                        a.wait_ge(c_sm.h, base_sm + 3 * tt + 1)
                        c_sm.inc(a.activation(expal[:], pb[6 + tt % 2][:, 0:KEXP],
                                              AF.Exp, bias=negm[:],
                                              accum_out=sume[:]))
            blk.scalar(a_2b)

            def v_2b(v, base_tr=base_tr, base_al=base_al, base_sm=base_sm,
                     base_vxc=base_vxc, base_at=base_at, base_ab=base_ab,
                     base_vx=base_vx, base_alpha=base_alpha,
                     base_dat2=(base_dat,)):
                for tt in range(NTT):
                    for ci in range(NCH):
                        n = tt * NCH + ci
                        v.wait_ge(c_tr.h, base_tr + n + 1)
                        c_trE.inc(v.tensor_copy(
                            h1T[:, ci * HALF + tt * 128:
                                ci * HALF + tt * 128 + 128],
                            pb[2 + n % 2][:, 0:128]))
                for tqb in range(2 if sub >= 3 else 0):
                    for pc in range(4 if sub >= 3 else 0):
                        j3 = tqb * 4 + pc
                        v.wait_ge(c_vxc.h, base_vxc + j3 + 1)
                        c_vx.inc(v.tensor_copy(vx[:, pc * 512:(pc + 1) * 512],
                                               pb[4 + j3 % 2][:, :]))
                    for ttl in range(4 if sub >= 4 else 0):
                        tt = tqb * 4 + ttl
                        v.wait_ge(c_al.h, base_al + tt + 1)
                        if tt >= 1:
                            v.wait_ge(c_sm.h, base_sm + 3 * (tt - 1) + 2)
                        c_sm.inc(v.reduce_max(negm[:], pb[6 + tt % 2][:, 0:KEXP],
                                              axis=AX, negate=True))
                        v.wait_ge(c_sm.h, base_sm + 3 * tt + 2)
                        c_sm.inc(v.reciprocal(rse[:], sume[:]))
                        v.wait_ge(c_sm.h, base_sm + 3 * tt + 3)
                        c_alpha.inc(v.tensor_scalar_mul(
                            alpha[:, tt * KEXP:(tt + 1) * KEXP], expal[:],
                            rse[:]))
                    if sub < 5:
                        continue
                    v.wait_ge(c_at.h, base_at + 4 * (tqb + 1))
                    if tqb >= 1:
                        v.wait_ge(d_at.h, base_dat2[0] + 16)
                    c_atE.inc(v.tensor_copy(at8[:, :], pb[2][0:KEXP, 0:512]))
                    for pc in range(4 if sub >= 6 else 0):
                        j4 = tqb * 4 + pc
                        v.wait_ge(c_ab.h, base_ab + j4 + 1)
                        v.wait_ge(c_vx.h, base_vx + 4 * (tqb + 1))
                        c_wvx.inc(v.tensor_mul(
                            wvx[:, pc * HALF + tqb * 512:
                                pc * HALF + tqb * 512 + 512],
                            pb[pc % 2][:, :],
                            vx[:, pc * 512:(pc + 1) * 512]))
            blk.vector(v_2b)

            def sy_2b(sy, base_atE=base_atE, base_dat=base_dat,
                      base_ab=base_ab):
                for tqb in range(2 if sub >= 5 else 0):
                    sy.wait_ge(c_atE.h, base_atE + tqb + 1)
                    sy.dma_start(out=at_scr[:], in_=at8[:]).then_inc(d_at.h, 16)
                    d_at.n += 16
                    sy.wait_ge(d_at.h, base_dat + 32 * tqb + 16)
                    if sub >= 6 and tqb >= 1:
                        sy.wait_ge(c_ab.h, base_ab + 4)
                    sy.dma_start(out=atf[:].rearrange("o (k t) -> o k t", t=512),
                                 in_=at_scr[None, :, :]).then_inc(d_at.h, 16)
                    d_at.n += 16
            blk.sync(sy_2b)

            barrier()
            if lvl == 6:
                def g_dump(g):
                    g.dma_start(out=dbg[:, 0:4 * HALF], in_=wvx[:]).then_inc(d_out.h, 16)
                    g.dma_start(out=dbg[:, 5000:5000 + NTT * KEXP],
                                in_=alpha[:]).then_inc(d_out.h, 16)
                    g.dma_start(out=dbg[:, 6000:6000 + NCH * HALF],
                                in_=h1T[:]).then_inc(d_out.h, 16)
                    d_out.n += 48
                    g.wait_ge(d_out.h, d_out.n)
                blk.gpsimd(g_dump)
                return nc
            sc2b.close()

            sc2c = ExitStack()
            wu_sb = sc2c.enter_context(nc.sbuf_tensor(f"wu_{half}", [P, 4 * C],
                                                      BF16))
            res2 = [sc2c.enter_context(nc.sbuf_tensor(f"res2_{half}_{i}",
                                                      [P, C], F32))
                    for i in range(2)]
            sqs_c = sc2c.enter_context(nc.sbuf_tensor(f"sqs_c{half}", [P, C],
                                                      BF16))

            base_ffn = c_ffn.n
            base_r2 = c_r2.n
            base_out = d_out.n
            base_wc = d_w.n
            base_lnc = c_ln.n

            def g_2c(g):
                src = wu.rearrange("(n p) d -> p n d", p=P)
                dst = wu_sb[:].rearrange("p (n d) -> p n d", d=C)
                g.dma_start(out=dst[:, 0:2, :], in_=src[:, 0:2, :]).then_inc(
                    d_w.h, 16)
                g.dma_start(out=dst[:, 2:4, :], in_=src[:, 2:4, :]).then_inc(
                    d_w.h, 16)
                d_w.n += 32
            blk.gpsimd(g_2c)

            wvx_done = c_wvx.n

            def t_2c(t, base_wc=base_wc, base_r2=base_r2, wvx_done=wvx_done):
                t.wait_ge(d_w.h, base_wc + 32)
                t.wait_ge(c_wvx.h, wvx_done)
                for tt in range(NTT):
                    for dblk in range(4):
                        n = tt * 4 + dblk
                        if n >= 2:
                            t.wait_ge(c_r2.h, base_r2 + n - 1)
                        for pc in range(4):
                            ins = t.matmul(pb[n % 2][:, :],
                                           wvx[:, pc * HALF + tt * 128:
                                               pc * HALF + tt * 128 + 128],
                                           wu_sb[:, pc * C + dblk * 512:
                                                 pc * C + (dblk + 1) * 512],
                                           start=(pc == 0), stop=(pc == 3))
                        c_ffn.inc(ins)
            blk.tensor(t_2c)

            def mk_v2c(tt, base_ffn=base_ffn, base_out=base_out):
                def v_2c(v):
                    for dblk in range(4):
                        n = tt * 4 + dblk
                        v.wait_ge(c_ffn.h, base_ffn + n + 1)
                        if subc >= 3 and tt >= 2 and dblk == 0:
                            v.wait_ge(d_out.h, base_out + 16 * (tt - 1))
                        c_r2.inc(v.tensor_add(
                            res2[tt % 2][:, dblk * 512:(dblk + 1) * 512],
                            pb[n % 2][:, :],
                            h1[:, tt * C + dblk * 512: tt * C + (dblk + 1) * 512]))
                return v_2c

            for tt in range(NTT):
                blk.vector(mk_v2c(tt))
                if subc >= 2:
                    ln_tile(res2[tt % 2][:, :], c_r2, base_r2 + tt * 4 + 4,
                            sqs_c[:])

            def sy_2c(sy, half=half, base_lnc=base_lnc):
                for tt in range(NTT if subc >= 3 else 0):
                    sy.wait_ge(c_ln.h, base_lnc + tt + 1)
                    sy.dma_start(out=out[(half * NTT + tt) * 128:
                                         (half * NTT + tt + 1) * 128, :],
                                 in_=res2[tt % 2][:, :]).then_inc(d_out.h, 16)
                    d_out.n += 16
            blk.sync(sy_2c)

            barrier()
            if lvl == 7:
                def sy_dump(sy):
                    if subc >= 3:
                        sy.wait_ge(d_out.h, d_out.n)
                    else:
                        sy.wait_ge(c_r2.h, c_r2.n)
                blk.sync(sy_dump)
                return nc
            sc2c.close()

        def sy_fin(sy):
            sy.wait_ge(d_out.h, d_out.n)
        blk.sync(sy_fin)

        stack.close()

    return nc


_NC_CACHE = {}


def kernel(L0_attn_x, L0_attn_Wq, L0_attn_Wk, L0_attn_Wv, L0_attn_Wo,
           L0_ffn_U, L0_ffn_V, L0_ffn_W1, L0_ffn_b1, L0_ffn_W2, L0_ffn_b2):
    x = np.asarray(L0_attn_x, dtype=np.float32)
    Wq = np.asarray(L0_attn_Wq, dtype=np.float32)
    Wk = np.asarray(L0_attn_Wk, dtype=np.float32)
    Wv = np.asarray(L0_attn_Wv, dtype=np.float32)
    Wo = np.asarray(L0_attn_Wo, dtype=np.float32)
    U = np.asarray(L0_ffn_U, dtype=np.float32)
    V = np.asarray(L0_ffn_V, dtype=np.float32)
    W1 = np.asarray(L0_ffn_W1, dtype=np.float32)
    b1 = np.asarray(L0_ffn_b1, dtype=np.float32)
    W2 = np.asarray(L0_ffn_W2, dtype=np.float32)
    b2 = np.asarray(L0_ffn_b2, dtype=np.float32)

    xT = np.ascontiguousarray(x.transpose(0, 2, 1))
    WqT = np.ascontiguousarray(Wq.T)
    WkT = np.ascontiguousarray(Wk.T)
    WvT = np.ascontiguousarray(Wv.T)
    woT = np.ascontiguousarray(Wo.T)
    wv2 = np.ascontiguousarray(V.transpose(2, 0, 1).reshape(C, KEXP * RK))
    wu = np.ascontiguousarray(U.transpose(0, 2, 1).reshape(KEXP * RK, C))
    b1m = np.ascontiguousarray(b1.reshape(NCH, P).T)
    b2r = np.ascontiguousarray(b2.reshape(1, KEXP))

    if "nc" not in _NC_CACHE:
        _NC_CACHE["nc"] = build_program()
    nc = _NC_CACHE["nc"]

    in_maps = []
    for c in range(8):
        bc = c // 4
        j = c % 4
        lo, hi = j, 7 - j
        hs = slice(2 * c * 128, 2 * c * 128 + 256)
        xres_c = np.ascontiguousarray(
            np.concatenate([x[bc, lo * 1024:(lo + 1) * 1024],
                            x[bc, hi * 1024:(hi + 1) * 1024]], axis=0))
        info_c = np.array([[bc * 256 * T + lo * 1024,
                            bc * 256 * T + hi * 1024, 0, 0]], dtype=np.uint32)
        in_maps.append({
            "xT": xT,
            "wq": np.ascontiguousarray(WqT[:, hs]),
            "wk": np.ascontiguousarray(WkT[:, hs]),
            "wv": np.ascontiguousarray(WvT[:, hs]),
            "wo": woT, "w1": W1, "b1m": b1m, "w2": W2, "b2": b2r,
            "wv2": wv2, "wu": wu, "xres": xres_c, "info": info_c,
        })

    res = run_bass_kernel_spmd(nc, in_maps, core_ids=list(range(8)))

    out_full = np.empty((B, T, C), dtype=np.float32)
    for c in range(8):
        bc = c // 4
        j = c % 4
        lo, hi = j, 7 - j
        o = res.results[c]["out"]
        out_full[bc, lo * 1024:(lo + 1) * 1024] = o[0:1024]
        out_full[bc, hi * 1024:(hi + 1) * 1024] = o[1024:2048]
    return out_full
